# version 03
# baseline (speedup 1.0000x reference)
import numpy as np
import ml_dtypes
import concourse.bass as bass
import concourse.mybir as mybir
from concourse.bass_utils import run_bass_kernel_spmd
from contextlib import ExitStack

F32 = mybir.dt.float32
BF16 = mybir.dt.bfloat16
U32 = mybir.dt.uint32
AF = mybir.ActivationFunctionType
ALU = mybir.AluOpType
AX = mybir.AxisListType

T = 4096
D = 1024
NH = 16
HD = 64
NCORES = 8

GP_DMA, HW_DMA, CC, ENG = 0, 1, 2, 3
INCS = {GP_DMA: 16, HW_DMA: 16, CC: 16, ENG: 1}
ENGINES = ("tensor", "vector", "scalar", "gpsimd", "sync")


class Chain:
    def __init__(self):
        self.ops = []
        self.tot = [0, 0, 0, 0]
        self._group_base = None

    def _add(self, engine, fn, kind):
        wait = list(self._group_base if self._group_base is not None else self.tot)
        self.ops.append((engine, fn, kind, wait))
        self.tot[kind] += INCS[kind]

    def dma(self, engine, out, in_, **kw):
        kind = GP_DMA if engine == "gpsimd" else HW_DMA
        self._add(engine, lambda e: e.dma_start(out=out, in_=in_, **kw), kind)

    def dma_fn(self, engine, fn):
        kind = GP_DMA if engine == "gpsimd" else HW_DMA
        self._add(engine, fn, kind)

    def dma_transpose(self, out, in_):
        self._add("sync", lambda e: e.dma_start_transpose(out, in_), HW_DMA)

    def op(self, engine, fn):
        self._add(engine, fn, ENG)

    def cc(self, fn):
        self._add("gpsimd", fn, CC)

    def group(self):
        chain = self

        class _G:
            def __enter__(self):
                chain._group_base = list(chain.tot)

            def __exit__(self, *a):
                chain._group_base = None
                return False

        return _G()

    def emit(self, block, sems):
        per_engine = {e: [] for e in ENGINES}
        for engine, fn, kind, wait in self.ops:
            per_engine[engine].append((fn, kind, wait))

        def make_body(eng_ops):
            def body(eng):
                cur = [0, 0, 0, 0]
                for fn, kind, wait in eng_ops:
                    for k in range(4):
                        if wait[k] > cur[k]:
                            eng.wait_ge(sems[k], wait[k])
                            cur[k] = wait[k]
                    inst = fn(eng)
                    if inst is None:
                        inst = eng.nop()
                    inst.then_inc(sems[kind], INCS[kind])
            return body

        for engine in ENGINES:
            if per_engine[engine]:
                getattr(block, engine)(make_body(per_engine[engine]))
        tot = list(self.tot)

        def fin(eng):
            for k in range(4):
                if tot[k] > 0:
                    eng.wait_ge(sems[k], tot[k])

        block.sync(fin)


def alloc_sb(nc, st, name, shape, dt=F32):
    return st.enter_context(nc.sbuf_tensor(name, shape, dt))


def alloc_ps(nc, st, name, shape, dt=F32):
    return st.enter_context(nc.psum_tensor(name, shape, dt))


def emit_ln_tok(nc, ch, st, pre, x_src_dram, ln_dst_dram, res_add_dram=None,
                x_out_dram=None, n_tiles=None):
    xt = alloc_sb(nc, st, pre + "xt", [128, D])
    bnb = alloc_sb(nc, st, pre + "bnb", [128, 12])
    mv = alloc_sb(nc, st, pre + "mv", [128, 2])
    t1 = alloc_sb(nc, st, pre + "t1", [128, 1])
    rr = alloc_sb(nc, st, pre + "rr", [128, 1])
    mb = alloc_sb(nc, st, pre + "mb", [128, 1])
    lnb = alloc_sb(nc, st, pre + "lnb", [128, D], BF16)
    epsb = alloc_sb(nc, st, pre + "epsb", [128, 1])
    if n_tiles is None:
        n_tiles = T // 128
    ch.op("vector", lambda v: v.memset(epsb[:], 1e-5))
    if res_add_dram is not None:
        resb = alloc_sb(nc, st, pre + "resb", [128, D])
        outb = alloc_sb(nc, st, pre + "outb", [128, D])
    for i in range(n_tiles):
        r0 = i * 128
        ch.dma("gpsimd", xt[:], x_src_dram[r0:r0 + 128, :])
        ch.op("vector", lambda v: v.bn_stats(out=bnb[:, 0:6], in_=xt[:, 0:512]))
        ch.op("vector", lambda v: v.bn_stats(out=bnb[:, 6:12], in_=xt[:, 512:1024]))
        ch.op("vector", lambda v: v.bn_aggr(out=mv[:], in_=bnb[:]))
        ch.op("scalar", lambda s: s.activation(out=t1[:], in_=mv[:, 1:2], func=AF.Sqrt, bias=epsb[:, 0:1]))
        ch.op("vector", lambda v: v.reciprocal(out=rr[:], in_=t1[:]))
        ch.op("vector", lambda v: v.tensor_scalar(
            out=mb[:], in0=mv[:, 0:1], scalar1=rr[:, 0:1], scalar2=-1.0,
            op0=ALU.mult, op1=ALU.mult))
        if ln_dst_dram is not None:
            ch.op("vector", lambda v: v.tensor_scalar(
                out=lnb[:], in0=xt[:], scalar1=rr[:, 0:1], scalar2=mb[:, 0:1],
                op0=ALU.mult, op1=ALU.add))
            ch.dma("gpsimd", ln_dst_dram[r0:r0 + 128, :], lnb[:])
        if res_add_dram is not None:
            ch.dma("gpsimd", resb[:], res_add_dram[r0:r0 + 128, :])
            ch.op("vector", lambda v: v.tensor_scalar(
                out=outb[:], in0=xt[:], scalar1=rr[:, 0:1], scalar2=mb[:, 0:1],
                op0=ALU.mult, op1=ALU.add))
            ch.op("vector", lambda v: v.tensor_tensor(
                out=outb[:], in0=outb[:], in1=resb[:], op=ALU.add))
            ch.dma("gpsimd", x_out_dram[r0:r0 + 128, :], outb[:])


def emit_transpose_to_feat(nc, ch, tok_dram_bf16, xT_sb, rows=None):
    if rows is None:
        rows = T
    for c in range(len(xT_sb)):
        ch.dma_transpose(xT_sb[c][:, 0:rows], tok_dram_bf16[0:rows, c * 128:(c + 1) * 128])


def emit_l0_attn(nc, ch, st, x_dram, w_attn_dram, w_proj_dram, x1_dram, scratch):
    ln_dram = scratch["ln"]
    qkT_dram = scratch["qkT"]
    v_dram = scratch["v"]
    oT_dram = scratch["oT"]
    y_dram = scratch["y"]
    nb_dram = scratch["nb"]

    with ExitStack() as st1:
        emit_ln_tok(nc, ch, st1, "l0a_ln1", x_dram, ln_dram)

    with ExitStack() as st2:
        xT = [alloc_sb(nc, st2, f"l0a_xT{c}", [128, T], BF16) for c in range(8)]
        emit_transpose_to_feat(nc, ch, ln_dram, xT)

        wat = [alloc_sb(nc, st2, f"l0a_w{c}", [128, 3072], BF16) for c in range(8)]
        for c in range(8):
            ch.dma("gpsimd", wat[c][:], w_attn_dram[c * 128:(c + 1) * 128, :])

        ps = alloc_ps(nc, st2, "l0a_ps", [128, 512])
        evb = alloc_sb(nc, st2, "l0a_evb", [128, 512], BF16)
        sqb = alloc_sb(nc, st2, "l0a_sqb", [128, 512], BF16)
        onesAB = alloc_sb(nc, st2, "l0a_ones", [128, 2], BF16)
        psn = alloc_ps(nc, st2, "l0a_psn", [2, 512])
        nrow = alloc_sb(nc, st2, "l0a_nrow", [2, 1])
        mxq = alloc_sb(nc, st2, "l0a_mxq", [2, 8])
        mxk = alloc_sb(nc, st2, "l0a_mxk", [2, 8])
        ch.op("vector", lambda v: v.memset(onesAB[:, :], 0.0))
        ch.op("vector", lambda v: v.memset(onesAB[0:64, 0:1], 1.0))
        ch.op("vector", lambda v: v.memset(onesAB[64:128, 1:2], 1.0))
        ch.op("vector", lambda v: v.memset(mxq[:], 0.0))
        ch.op("vector", lambda v: v.memset(mxk[:], 0.0))

        for (qk, col0, mx) in (("k", D, mxk), ("q", 0, mxq)):
            scale = 0.125 if qk == "q" else 1.0
            for j in range(8):
                for t in range(T // 512):
                    for c in range(8):
                        ch.op("tensor", lambda tt, c=c, j=j, t=t, col0=col0: tt.matmul(
                            out=ps[:], lhsT=wat[c][:, col0 + j * 128:col0 + (j + 1) * 128],
                            rhs=xT[c][:, t * 512:(t + 1) * 512],
                            start=(c == 0), stop=(c == 7)))
                    ch.op("scalar", lambda s, scale=scale: s.activation(
                        out=evb[:], in_=ps[:], func=AF.Copy, scale=scale))
                    ch.dma("gpsimd", qkT_dram[col0 + j * 128:col0 + (j + 1) * 128,
                                              t * 512:(t + 1) * 512], evb[:])
                    ch.op("scalar", lambda s, scale=scale: s.activation(
                        out=sqb[:], in_=ps[:], func=AF.Square, scale=scale))
                    ch.op("tensor", lambda tt: tt.matmul(out=psn[:], lhsT=onesAB[:], rhs=sqb[:],
                                                         start=True, stop=True))
                    ch.op("vector", lambda v: v.tensor_reduce(out=nrow[:], in_=psn[:],
                                                              axis=AX.X, op=ALU.max))
                    ch.op("vector", lambda v, mx=mx, j=j: v.tensor_tensor(
                        out=mx[:, j:j + 1], in0=mx[:, j:j + 1], in1=nrow[:], op=ALU.max))

        for g in range(2):
            for ti in range(T // 128):
                for c in range(8):
                    ch.op("tensor", lambda tt, c=c, ti=ti, g=g: tt.matmul(
                        out=ps[:], lhsT=xT[c][:, ti * 128:(ti + 1) * 128],
                        rhs=wat[c][:, 2 * D + g * 512: 2 * D + (g + 1) * 512],
                        start=(c == 0), stop=(c == 7)))
                ch.op("scalar", lambda s: s.copy(out=evb[:], in_=ps[:]))
                ch.dma("gpsimd", v_dram[ti, 8 * g:8 * (g + 1), :, :].rearrange("h p c -> p h c"),
                       evb[:])

        ch.dma("gpsimd", nb_dram[0], mxq[:])
        ch.dma("gpsimd", nb_dram[1], mxk[:])

    with ExitStack() as st3:
        nb16 = alloc_sb(nc, st3, "l0a_nb16", [1, 32])
        negc1 = alloc_sb(nc, st3, "l0a_negc1", [1, 16])
        ones_r = alloc_sb(nc, st3, "l0a_ones_r", [1, 128])
        biascol = alloc_sb(nc, st3, "l0a_biascol", [128, 1])
        psb = alloc_ps(nc, st3, "l0a_psb", [128, 1])
        qt = alloc_sb(nc, st3, "l0a_qt", [64, T], BF16)
        kt = alloc_sb(nc, st3, "l0a_kt", [64, T], BF16)
        vt = alloc_sb(nc, st3, "l0a_vt", [128, (T // 128) * 65], BF16)
        rd = alloc_sb(nc, st3, "l0a_rd", [1, 512])
        ones64 = alloc_sb(nc, st3, "l0a_ones64", [1, 64])
        psbc = alloc_ps(nc, st3, "l0a_psbc", [64, 512])
        bcb = alloc_sb(nc, st3, "l0a_bcb", [64, 512])
        obn = alloc_sb(nc, st3, "l0a_obn", [64, 512], BF16)
        et = [alloc_sb(nc, st3, f"l0a_et{i}", [128, 512], BF16) for i in range(T // 128)]
        masks = alloc_sb(nc, st3, "l0a_mask", [128, 4 * 512], BF16)
        colid = alloc_sb(nc, st3, "l0a_colid", [128, 512])
        rowid = alloc_sb(nc, st3, "l0a_rowid", [128, 1])
        pss = alloc_ps(nc, st3, "l0a_pss", [128, 512])
        pso = alloc_ps(nc, st3, "l0a_pso", [65, 512])
        ob = alloc_sb(nc, st3, "l0a_ob", [65, 512])

        ch.dma("gpsimd", nb16[:], nb_dram.rearrange("a b c -> (a b c)").rearrange("(x y) -> x y", x=1))
        ch.op("scalar", lambda s: s.activation(out=nb16[:], in_=nb16[:], func=AF.Sqrt))
        ch.op("vector", lambda v: v.tensor_tensor(
            out=negc1[:], in0=nb16[0:1, 0:16], in1=nb16[0:1, 16:32], op=ALU.mult))
        ch.op("vector", lambda v: v.tensor_scalar(
            out=negc1[:], in0=negc1[:], scalar1=-1.0, scalar2=-1.0,
            op0=ALU.mult, op1=ALU.add))
        ch.op("vector", lambda v: v.memset(ones_r[:], 1.0))
        ch.op("vector", lambda v: v.memset(ones64[:], 1.0))

        ch.op("gpsimd", lambda g: g.iota(colid[:], pattern=[[1, 512]], base=0,
                                         channel_multiplier=0,
                                         allow_small_or_imprecise_dtypes=True))
        ch.op("gpsimd", lambda g: g.iota(rowid[:], pattern=[[0, 1]], base=0,
                                         channel_multiplier=1,
                                         allow_small_or_imprecise_dtypes=True))
        for i in range(4):
            ch.op("vector", lambda v, i=i: v.tensor_scalar(
                out=masks[:, i * 512:(i + 1) * 512], in0=colid[:],
                scalar1=float(128 * i), scalar2=rowid[:, 0:1],
                op0=ALU.subtract, op1=ALU.is_ge))

        for h in range(NH):
            hflat = (h % 2) * 8 + h // 2
            ch.op("tensor", lambda tt, hflat=hflat: tt.matmul(
                out=psb[:], lhsT=ones_r[:], rhs=negc1[0:1, hflat:hflat + 1],
                start=True, stop=True))
            ch.op("vector", lambda v: v.tensor_copy(out=biascol[:], in_=psb[:]))

            ch.dma("gpsimd", qt[:], qkT_dram[h * 64:(h + 1) * 64, :])
            ch.dma("gpsimd", kt[:], qkT_dram[D + h * 64:D + (h + 1) * 64, :])
            ch.op("vector", lambda v: v.memset(vt[:], 1.0))
            ch.dma("gpsimd", vt[:].rearrange("p (k c) -> p k c", c=65)[:, :, 0:64],
                   v_dram[:, h, :, :].rearrange("k p c -> p k c"))
            for j in range(T // 512):
                nkt = 4 * (j + 1)
                for i in range(nkt):
                    ch.op("tensor", lambda tt, i=i, j=j: tt.matmul(
                        out=pss[:], lhsT=kt[:, i * 128:(i + 1) * 128],
                        rhs=qt[:, j * 512:(j + 1) * 512], start=True, stop=True))
                    ch.op("scalar", lambda s, i=i: s.activation(
                        out=et[i][:], in_=pss[:], func=AF.Exp, bias=biascol[:, 0:1]))
                    if i >= 4 * j:
                        mi = i - 4 * j
                        ch.op("vector", lambda v, i=i, mi=mi: v.tensor_tensor(
                            out=et[i][:], in0=et[i][:],
                            in1=masks[:, mi * 512:(mi + 1) * 512], op=ALU.mult))
                for i in range(nkt):
                    ch.op("tensor", lambda tt, i=i, nkt=nkt: tt.matmul(
                        out=pso[:], lhsT=vt[:, i * 65:(i + 1) * 65], rhs=et[i][:],
                        start=(i == 0), stop=(i == nkt - 1)))
                ch.op("vector", lambda v: v.tensor_copy(out=ob[:], in_=pso[:]))
                ch.op("vector", lambda v: v.reciprocal(out=rd[:], in_=ob[64:65, :]))
                ch.op("tensor", lambda tt: tt.matmul(out=psbc[:], lhsT=ones64[:], rhs=rd[:],
                                                     start=True, stop=True))
                ch.op("vector", lambda v: v.tensor_copy(out=bcb[:], in_=psbc[:]))
                ch.op("vector", lambda v: v.tensor_tensor(out=obn[:], in0=ob[0:64, :],
                                                          in1=bcb[:], op=ALU.mult))
                ch.dma("gpsimd", oT_dram[h * 64:(h + 1) * 64, j * 512:(j + 1) * 512], obn[:])

    with ExitStack() as st4:
        oT = [alloc_sb(nc, st4, f"l0a_oT{c}", [128, T], BF16) for c in range(8)]
        for c in range(8):
            ch.dma("gpsimd", oT[c][:], oT_dram[c * 128:(c + 1) * 128, :])
        wp = [alloc_sb(nc, st4, f"l0a_wp{c}", [128, D], BF16) for c in range(8)]
        for c in range(8):
            ch.dma("gpsimd", wp[c][:], w_proj_dram[c * 128:(c + 1) * 128, :])
        ps2 = alloc_ps(nc, st4, "l0a_ps2", [128, 512])
        yb = alloc_sb(nc, st4, "l0a_yb", [128, 512])
        for ti in range(T // 128):
            for jc in range(2):
                for c in range(8):
                    ch.op("tensor", lambda tt, c=c, ti=ti, jc=jc: tt.matmul(
                        out=ps2[:], lhsT=oT[c][:, ti * 128:(ti + 1) * 128],
                        rhs=wp[c][:, jc * 512:(jc + 1) * 512],
                        start=(c == 0), stop=(c == 7)))
                ch.op("vector", lambda v: v.tensor_copy(out=yb[:], in_=ps2[:]))
                ch.dma("gpsimd", y_dram[ti * 128:(ti + 1) * 128,
                                        jc * 512:(jc + 1) * 512], yb[:])

    with ExitStack() as st5:
        emit_ln_tok(nc, ch, st5, "l0a_ln2", y_dram, None, res_add_dram=x_dram,
                    x_out_dram=x1_dram)


def build_launch_l0attn():
    nc = bass.Bass("TRN2", target_bir_lowering=False)
    x_in = nc.dram_tensor("x", [T, D], F32, kind="ExternalInput").ap()
    w_attn = nc.dram_tensor("w_attn", [D, 3 * D], F32, kind="ExternalInput").ap()
    w_proj = nc.dram_tensor("w_proj", [D, D], F32, kind="ExternalInput").ap()
    x1 = nc.dram_tensor("x1", [T, D], F32, kind="ExternalOutput").ap()
    scratch = {
        "ln": nc.dram_tensor("s_ln", [T, D], BF16).ap(),
        "qkT": nc.dram_tensor("s_qkT", [2 * D, T], BF16).ap(),
        "v": nc.dram_tensor("s_v", [T // 128, 16, 128, 64], BF16).ap(),
        "oT": nc.dram_tensor("s_oT", [D, T], BF16).ap(),
        "y": nc.dram_tensor("s_y", [T, D], F32).ap(),
        "nb": nc.dram_tensor("s_nb", [2, 2, 8], F32).ap(),
    }
    ch = Chain()
    with ExitStack() as st:
        sems = [st.enter_context(nc.semaphore(n)) for n in ("s_gp", "s_hw", "s_cc", "s_eng")]
        block = st.enter_context(nc.Block())
        emit_l0_attn(nc, ch, st, x_in, w_attn, w_proj, x1, scratch)
        ch.emit(block, sems)
    return nc


E0 = 64
K0 = 6
CAP = 384
NSLOT = E0 * CAP
ROWW = 1040


def emit_l0ffn_routing(nc, ch, st, x1_dram, rw_dram, eb_dram, ln1_dram,
                       idx6_dram, w64_dram, totals_dram, scratch):
    with ExitStack() as st1:
        emit_ln_tok(nc, ch, st1, "l0f_ln", x1_dram, ln1_dram)

    with ExitStack() as st2:
        lnT = [alloc_sb(nc, st2, f"l0f_lnT{c}", [128, T], BF16) for c in range(8)]
        emit_transpose_to_feat(nc, ch, ln1_dram, lnT)
        rwb_dram = scratch["rwb"]
        ch.dma("gpsimd", rwb_dram, rw_dram)
        rwT = [alloc_sb(nc, st2, f"l0f_rwT{c}", [128, E0], BF16) for c in range(8)]
        for c in range(8):
            ch.dma_transpose(rwT[c][:], rwb_dram[:, c * 128:(c + 1) * 128])
        ebrow = alloc_sb(nc, st2, "l0f_ebrow", [1, E0])
        ones_r = alloc_sb(nc, st2, "l0f_ones_r", [1, 128])
        psb = alloc_ps(nc, st2, "l0f_psb", [128, E0])
        ebB = alloc_sb(nc, st2, "l0f_ebB", [128, E0])
        ch.dma("gpsimd", ebrow[:], eb_dram.rearrange("(x e) -> x e", x=1))
        ch.op("vector", lambda v: v.memset(ones_r[:], 1.0))
        ch.op("tensor", lambda t: t.matmul(out=psb[:], lhsT=ones_r[:], rhs=ebrow[:],
                                           start=True, stop=True))
        ch.op("vector", lambda v: v.tensor_copy(out=ebB[:], in_=psb[:]))

        iotaE = alloc_sb(nc, st2, "l0f_iotaE", [128, E0])
        ch.op("gpsimd", lambda g: g.iota(iotaE[:], pattern=[[1, E0]], base=0,
                                         channel_multiplier=0,
                                         allow_small_or_imprecise_dtypes=True))
        psl = alloc_ps(nc, st2, "l0f_psl", [128, E0])
        pst = alloc_ps(nc, st2, "l0f_pst", [E0, 1])
        scb = alloc_sb(nc, st2, "l0f_scb", [128, E0])
        corr = alloc_sb(nc, st2, "l0f_corr", [128, E0])
        mx8 = alloc_sb(nc, st2, "l0f_mx8", [128, 8])
        mi8 = alloc_sb(nc, st2, "l0f_mi8", [128, 8], U32)
        mif = alloc_sb(nc, st2, "l0f_mif", [128, 8])
        ohr = alloc_sb(nc, st2, "l0f_ohr", [128, E0])
        amat = alloc_sb(nc, st2, "l0f_amat", [128, E0])
        amatb = alloc_sb(nc, st2, "l0f_amatb", [128, E0], BF16)
        w64 = alloc_sb(nc, st2, "l0f_w64", [128, E0])
        tsum = alloc_sb(nc, st2, "l0f_tsum", [128, 1])
        trec = alloc_sb(nc, st2, "l0f_trec", [128, 1])
        onescol = alloc_sb(nc, st2, "l0f_onescol", [128, 1], BF16)
        totsb = alloc_sb(nc, st2, "l0f_totsb", [E0, 1])
        ch.op("vector", lambda v: v.memset(onescol[:], 1.0))

        for ti in range(T // 128):
            for c in range(8):
                ch.op("tensor", lambda t, c=c, ti=ti: t.matmul(
                    out=psl[:], lhsT=lnT[c][:, ti * 128:(ti + 1) * 128], rhs=rwT[c][:],
                    start=(c == 0), stop=(c == 7)))
            ch.op("scalar", lambda s: s.activation(out=scb[:], in_=psl[:], func=AF.Sigmoid))
            ch.op("vector", lambda v: v.tensor_tensor(out=corr[:], in0=scb[:], in1=ebB[:],
                                                      op=ALU.add))
            ch.op("vector", lambda v: v.max(out=mx8[:], in_=corr[:]))
            ch.op("vector", lambda v: v.max_index(out=mi8[:], in_max=mx8[:], in_values=corr[:]))
            ch.op("vector", lambda v: v.tensor_copy(out=mif[:], in_=mi8[:]))
            for r in range(K0):
                if r == 0:
                    ch.op("vector", lambda v, r=r: v.tensor_scalar(
                        out=amat[:], in0=iotaE[:], scalar1=mif[:, r:r + 1], scalar2=None,
                        op0=ALU.is_equal))
                else:
                    ch.op("vector", lambda v, r=r: v.tensor_scalar(
                        out=ohr[:], in0=iotaE[:], scalar1=mif[:, r:r + 1], scalar2=None,
                        op0=ALU.is_equal))
                    ch.op("vector", lambda v: v.tensor_tensor(out=amat[:], in0=amat[:],
                                                              in1=ohr[:], op=ALU.add))
            ch.op("vector", lambda v: v.tensor_tensor(out=w64[:], in0=amat[:], in1=scb[:],
                                                      op=ALU.mult))
            ch.op("vector", lambda v: v.tensor_reduce(out=tsum[:], in_=w64[:],
                                                      axis=AX.X, op=ALU.add))
            ch.op("vector", lambda v: v.tensor_scalar(out=tsum[:], in0=tsum[:],
                                                      scalar1=1e-20, scalar2=None, op0=ALU.add))
            ch.op("vector", lambda v: v.reciprocal(out=trec[:], in_=tsum[:]))
            ch.op("vector", lambda v: v.tensor_scalar(out=w64[:], in0=w64[:],
                                                      scalar1=trec[:, 0:1], scalar2=None,
                                                      op0=ALU.mult))
            ch.op("vector", lambda v: v.tensor_copy(out=amatb[:], in_=amat[:]))
            ch.op("tensor", lambda t, ti=ti: t.matmul(out=pst[:], lhsT=amatb[:], rhs=onescol[:],
                                                      start=(ti == 0), stop=(ti == T // 128 - 1)))
            ch.dma("gpsimd", idx6_dram[ti * 128:(ti + 1) * 128, :], mi8[:])
            ch.dma("gpsimd", w64_dram[ti * 128:(ti + 1) * 128, :], w64[:])
        ch.op("vector", lambda v: v.tensor_copy(out=totsb[:], in_=pst[:]))
        ch.dma("gpsimd", totals_dram, totsb[:])


def emit_l0ffn_scatter(nc, ch, st, ln1_dram, idx6_dram, w64_dram, totals_all_dram,
                       prevmask_dram, slotbuf_dram, gidx_dram, cap=None):
    if cap is None:
        cap = CAP
    with ExitStack() as st2:
        pmsk = alloc_sb(nc, st2, "l0s_pmsk", [8, 1])
        tall = alloc_sb(nc, st2, "l0s_tall", [8, E0])
        psbase = alloc_ps(nc, st2, "l0s_psbase", [1, E0])
        ch.dma("gpsimd", pmsk[:], prevmask_dram)
        ch.dma("gpsimd", tall[:], totals_all_dram)
        ch.op("tensor", lambda t: t.matmul(out=psbase[:], lhsT=pmsk[:], rhs=tall[:],
                                           start=True, stop=True))
        baser = alloc_sb(nc, st2, "l0s_baser", [1, E0])
        ch.op("vector", lambda v: v.tensor_copy(out=baser[:], in_=psbase[:]))

        iotaE = alloc_sb(nc, st2, "l0s_iotaE", [128, E0])
        colid = alloc_sb(nc, st2, "l0s_colid", [128, 128])
        rowid = alloc_sb(nc, st2, "l0s_rowid", [128, 1])
        stri = alloc_sb(nc, st2, "l0s_stri", [128, 128])
        onesv = alloc_sb(nc, st2, "l0s_onesv", [1, 128])
        ch.op("gpsimd", lambda g: g.iota(iotaE[:], pattern=[[1, E0]], base=0,
                                         channel_multiplier=0,
                                         allow_small_or_imprecise_dtypes=True))
        ch.op("gpsimd", lambda g: g.iota(colid[:], pattern=[[1, 128]], base=0,
                                         channel_multiplier=0,
                                         allow_small_or_imprecise_dtypes=True))
        ch.op("gpsimd", lambda g: g.iota(rowid[:], pattern=[[0, 1]], base=0,
                                         channel_multiplier=1,
                                         allow_small_or_imprecise_dtypes=True))
        ch.op("vector", lambda v: v.tensor_scalar(out=stri[:], in0=colid[:],
                                                  scalar1=rowid[:, 0:1], scalar2=None,
                                                  op0=ALU.is_gt))
        ch.op("vector", lambda v: v.memset(onesv[:], 1.0))

        mi8 = alloc_sb(nc, st2, "l0s_mi8", [128, 8], U32)
        mif = alloc_sb(nc, st2, "l0s_mif", [128, 8])
        w64 = alloc_sb(nc, st2, "l0s_w64", [128, E0])
        ohr = alloc_sb(nc, st2, "l0s_ohr", [128, E0])
        amat = alloc_sb(nc, st2, "l0s_amat", [128, E0])
        tmpe = alloc_sb(nc, st2, "l0s_tmpe", [128, E0])
        pspos = alloc_ps(nc, st2, "l0s_pspos", [128, E0])
        posf = alloc_sb(nc, st2, "l0s_posf", [128, E0])
        valid = alloc_sb(nc, st2, "l0s_valid", [128, E0])
        wmask = alloc_sb(nc, st2, "l0s_wmask", [128, E0])
        lnb = alloc_sb(nc, st2, "l0s_lnb", [128, D], BF16)
        comb = alloc_sb(nc, st2, "l0s_comb", [128, ROWW], BF16)
        ch.op("vector", lambda v: v.memset(comb[:], 0.0))
        posr = alloc_sb(nc, st2, "l0s_posr", [128, 1])
        wr = alloc_sb(nc, st2, "l0s_wr", [128, 1])
        vr = alloc_sb(nc, st2, "l0s_vr", [128, 1], U32)
        slotf = alloc_sb(nc, st2, "l0s_slotf", [128, 1])
        ssel = alloc_sb(nc, st2, "l0s_ssel", [128, 1])
        invf = alloc_sb(nc, st2, "l0s_invf", [128, 1])
        slotu = alloc_sb(nc, st2, "l0s_slotu", [128, 1], U32)
        gslotf = alloc_sb(nc, st2, "l0s_gslotf", [128, 1])
        gslotu = alloc_sb(nc, st2, "l0s_gsu", [128, 6], U32)
        zrow = alloc_sb(nc, st2, "l0s_zrow", [128, 1])
        ch.op("vector", lambda v: v.memset(zrow[:], float(NSLOT)))
        rbase = alloc_sb(nc, st2, "l0s_rbase", [1, E0])
        psts = alloc_ps(nc, st2, "l0s_psts", [1, E0])
        tsrow = alloc_sb(nc, st2, "l0s_tsrow", [1, E0])
        onesc128 = alloc_sb(nc, st2, "l0s_onesc128", [128, 1])
        ch.op("vector", lambda v: v.memset(onesc128[:], 1.0))
        ch.op("vector", lambda v: v.tensor_copy(out=rbase[:], in_=baser[:]))

        for ti in range(T // 128):
            r0 = ti * 128
            ch.dma("gpsimd", mi8[:], idx6_dram[r0:r0 + 128, :])
            ch.dma("gpsimd", w64[:], w64_dram[r0:r0 + 128, :])
            ch.dma("gpsimd", lnb[:], ln1_dram[r0:r0 + 128, :])
            ch.op("vector", lambda v: v.tensor_copy(out=mif[:], in_=mi8[:]))
            for r in range(K0):
                if r == 0:
                    ch.op("vector", lambda v, r=r: v.tensor_scalar(
                        out=amat[:], in0=iotaE[:], scalar1=mif[:, r:r + 1], scalar2=None,
                        op0=ALU.is_equal))
                else:
                    ch.op("vector", lambda v, r=r: v.tensor_scalar(
                        out=ohr[:], in0=iotaE[:], scalar1=mif[:, r:r + 1], scalar2=None,
                        op0=ALU.is_equal))
                    ch.op("vector", lambda v: v.tensor_tensor(out=amat[:], in0=amat[:],
                                                              in1=ohr[:], op=ALU.add))
            ch.op("tensor", lambda t: t.matmul(out=pspos[:], lhsT=stri[:], rhs=amat[:],
                                               start=True, stop=False))
            ch.op("tensor", lambda t: t.matmul(out=pspos[:], lhsT=onesv[:], rhs=rbase[:],
                                               start=False, stop=True))
            ch.op("vector", lambda v: v.tensor_copy(out=posf[:], in_=pspos[:]))
            ch.op("vector", lambda v, cap=cap: v.tensor_scalar(
                out=valid[:], in0=posf[:], scalar1=float(cap), scalar2=None, op0=ALU.is_lt))
            ch.op("vector", lambda v: v.tensor_tensor(out=wmask[:], in0=w64[:], in1=valid[:],
                                                      op=ALU.mult))
            ch.op("tensor", lambda t: t.matmul(out=psts[:], lhsT=onesc128[:], rhs=amat[:],
                                               start=True, stop=True))
            ch.op("vector", lambda v: v.tensor_copy(out=tsrow[:], in_=psts[:]))
            ch.op("vector", lambda v: v.tensor_tensor(out=rbase[:], in0=rbase[:], in1=tsrow[:],
                                                      op=ALU.add))
            for r in range(K0):
                ch.op("vector", lambda v, r=r: v.tensor_scalar(
                    out=ohr[:], in0=iotaE[:], scalar1=mif[:, r:r + 1], scalar2=None,
                    op0=ALU.is_equal))
                ch.op("vector", lambda v: v.tensor_tensor(out=tmpe[:], in0=ohr[:], in1=posf[:],
                                                          op=ALU.mult))
                ch.op("vector", lambda v: v.tensor_reduce(out=posr[:], in_=tmpe[:],
                                                          axis=AX.X, op=ALU.add))
                ch.op("vector", lambda v: v.tensor_tensor(out=tmpe[:], in0=ohr[:], in1=wmask[:],
                                                          op=ALU.mult))
                ch.op("vector", lambda v: v.tensor_reduce(out=wr[:], in_=tmpe[:],
                                                          axis=AX.X, op=ALU.add))
                ch.op("vector", lambda v, cap=cap: v.tensor_scalar(
                    out=vr[:], in0=posr[:], scalar1=float(cap), scalar2=None, op0=ALU.is_lt))
                ch.op("vector", lambda v, r=r, cap=cap: v.tensor_scalar(
                    out=slotf[:], in0=mif[:, r:r + 1], scalar1=float(cap),
                    scalar2=posr[:, 0:1], op0=ALU.mult, op1=ALU.add))
                ch.op("vector", lambda v: v.tensor_scalar(
                    out=invf[:], in0=rowid[:], scalar1=float(NSLOT), scalar2=None,
                    op0=ALU.add))
                ch.op("vector", lambda v: v.select(out=ssel[:], mask=vr[:], on_true=slotf[:],
                                                   on_false=invf[:], add_drain=True))
                ch.op("vector", lambda v: v.tensor_copy(out=slotu[:], in_=ssel[:]))
                ch.op("vector", lambda v: v.select(out=gslotf[:], mask=vr[:], on_true=slotf[:],
                                                   on_false=zrow[:], add_drain=True))
                ch.op("vector", lambda v, r=r: v.tensor_copy(out=gslotu[:, r:r + 1], in_=gslotf[:]))
                ch.op("vector", lambda v: v.tensor_copy(out=comb[:, 0:D], in_=lnb[:]))
                ch.op("vector", lambda v: v.tensor_copy(out=comb[:, D:D + 1], in_=wr[:]))
                ch.dma_fn("gpsimd", lambda g: g.indirect_dma_start(
                    out=slotbuf_dram,
                    out_offset=bass.IndirectOffsetOnAxis(ap=slotu[:], axis=0),
                    in_=comb[:], in_offset=None))
            ch.dma("gpsimd", gidx_dram[r0:r0 + 128, :], gslotu[:])


GELU_C = 0.7978845608028654
GELU_A = 0.044715


def emit_gelu_tanh(nc, ch, gb, pin, out_bf, pre):
    g2, g3, inner, th = gb
    ch.op("scalar", lambda s: s.activation(out=g2[:], in_=pin[:], func=AF.Square))
    ch.op("vector", lambda v: v.tensor_tensor(out=g3[:], in0=g2[:], in1=pin[:], op=ALU.mult))
    ch.op("vector", lambda v: v.tensor_scalar(out=inner[:], in0=g3[:], scalar1=GELU_A,
                                              scalar2=None, op0=ALU.mult))
    ch.op("vector", lambda v: v.tensor_tensor(out=inner[:], in0=inner[:], in1=pin[:], op=ALU.add))
    ch.op("scalar", lambda s: s.activation(out=th[:], in_=inner[:], func=AF.Tanh, scale=GELU_C))
    ch.op("vector", lambda v: v.tensor_scalar(out=th[:], in0=th[:], scalar1=1.0, scalar2=0.5,
                                              op0=ALU.add, op1=ALU.mult))
    ch.op("vector", lambda v: v.tensor_tensor(out=out_bf[:], in0=th[:], in1=pin[:], op=ALU.mult))


def emit_l0ffn_experts(nc, ch, st, parts_dram, w1_dram, w2_dram, ys_dram, scratch):
    NR = 8 * CAP
    NTile = NR // 128
    xs_dram = scratch["xs"]
    with ExitStack() as st2:
        acc = alloc_sb(nc, st2, "l0e_acc", [128, ROWW])
        pb = alloc_sb(nc, st2, "l0e_pb", [128, ROWW], BF16)
        accb = alloc_sb(nc, st2, "l0e_accb", [128, D], BF16)
        wcol = alloc_sb(nc, st2, "l0e_wcol", [128, NTile])
        for s in range(NTile):
            r0 = s * 128
            for p in range(8):
                ch.dma("gpsimd", pb[:], parts_dram[p, r0:r0 + 128, :])
                if p == 0:
                    ch.op("vector", lambda v: v.tensor_copy(out=acc[:], in_=pb[:]))
                else:
                    ch.op("vector", lambda v: v.tensor_tensor(out=acc[:], in0=acc[:],
                                                              in1=pb[:], op=ALU.add))
            ch.op("vector", lambda v: v.tensor_copy(out=accb[:], in_=acc[:, 0:D]))
            ch.op("vector", lambda v, s=s: v.tensor_copy(out=wcol[:, s:s + 1],
                                                         in_=acc[:, D:D + 1]))
            ch.dma("gpsimd", xs_dram[r0:r0 + 128, :], accb[:])

        xsT = [alloc_sb(nc, st2, f"l0e_xsT{c}", [128, NR], BF16) for c in range(8)]
        for c in range(8):
            ch.dma_transpose(xsT[c][:], xs_dram[:, c * 128:(c + 1) * 128])

        w1b = [alloc_sb(nc, st2, f"l0e_w1b{c}", [128, 512], BF16) for c in range(8)]
        w2b = [alloc_sb(nc, st2, f"l0e_w2b{c}", [128, D], BF16) for c in range(4)]
        psh = alloc_ps(nc, st2, "l0e_psh", [128, CAP])
        hb = [alloc_sb(nc, st2, f"l0e_hb{i}", [128, CAP], BF16) for i in range(4)]
        g2 = alloc_sb(nc, st2, "l0e_g2", [128, CAP])
        g3 = alloc_sb(nc, st2, "l0e_g3", [128, CAP])
        inner = alloc_sb(nc, st2, "l0e_inner", [128, CAP])
        th = alloc_sb(nc, st2, "l0e_th", [128, CAP])
        psd = alloc_ps(nc, st2, "l0e_psd", [128, 512])
        yb = alloc_sb(nc, st2, "l0e_yb", [128, 512], BF16)

        for e in range(8):
            for c in range(8):
                ch.dma("gpsimd", w1b[c][:], w1_dram[e, c * 128:(c + 1) * 128, :])
            for c in range(4):
                ch.dma("gpsimd", w2b[c][:], w2_dram[e, c * 128:(c + 1) * 128, :])
            s0 = e * CAP
            for ic in range(4):
                for c in range(8):
                    ch.op("tensor", lambda t, c=c, ic=ic, s0=s0: t.matmul(
                        out=psh[:], lhsT=w1b[c][:, ic * 128:(ic + 1) * 128],
                        rhs=xsT[c][:, s0:s0 + CAP],
                        start=(c == 0), stop=(c == 7)))
                emit_gelu_tanh(nc, ch, (g2, g3, inner, th), psh, hb[ic], f"l0e_g{e}_{ic}")
            for stile in range(CAP // 128):
                for jc in range(2):
                    for ic in range(4):
                        ch.op("tensor", lambda t, ic=ic, stile=stile, jc=jc: t.matmul(
                            out=psd[:], lhsT=hb[ic][:, stile * 128:(stile + 1) * 128],
                            rhs=w2b[ic][:, jc * 512:(jc + 1) * 512],
                            start=(ic == 0), stop=(ic == 3)))
                    gt = e * (CAP // 128) + stile
                    ch.op("vector", lambda v, gt=gt: v.tensor_scalar(
                        out=yb[:], in0=psd[:], scalar1=wcol[:, gt:gt + 1], scalar2=None,
                        op0=ALU.mult))
                    ch.dma("gpsimd", ys_dram[s0 + stile * 128:s0 + (stile + 1) * 128,
                                             jc * 512:(jc + 1) * 512], yb[:])


def emit_l0ffn_combine(nc, ch, st, x1_dram, ln1_dram, gidx_dram, ysall_dram,
                       sw1_dram, sw2_dram, x2_dram, scratch):
    ysbuf = scratch["ysbuf"]
    y_dram = scratch["yffn"]
    sw1b_dram = scratch["sw1b"]
    sw2b_dram = scratch["sw2b"]
    hsT_dram = scratch["hsT"]

    with ExitStack() as st2:
        zb = alloc_sb(nc, st2, "l0c_zb", [128, D], BF16)
        ch.op("vector", lambda v: v.memset(zb[:], 0.0))
        ch.dma("gpsimd", ysbuf[0:NSLOT, :], ysall_dram)
        ch.dma("gpsimd", ysbuf[NSLOT:NSLOT + 1, :], zb[0:1, :])

        ch.dma("gpsimd", sw1b_dram, sw1_dram)
        ch.dma("gpsimd", sw2b_dram, sw2_dram)
        sw1T = [alloc_sb(nc, st2, f"l0c_sw1T{c}", [128, 2048], BF16) for c in range(8)]
        for c in range(8):
            ch.dma_transpose(sw1T[c][:], sw1b_dram[:, c * 128:(c + 1) * 128])

        lnT = [alloc_sb(nc, st2, f"l0c_lnT{c}", [128, T], BF16) for c in range(8)]
        emit_transpose_to_feat(nc, ch, ln1_dram, lnT)

        psh = alloc_ps(nc, st2, "l0c_psh", [128, 512])
        hbf = alloc_sb(nc, st2, "l0c_hbf", [128, 512], BF16)
        g2 = alloc_sb(nc, st2, "l0c_g2", [128, 512])
        g3 = alloc_sb(nc, st2, "l0c_g3", [128, 512])
        inner = alloc_sb(nc, st2, "l0c_inner", [128, 512])
        th = alloc_sb(nc, st2, "l0c_th", [128, 512])
        for fc in range(16):
            for t in range(T // 512):
                for c in range(8):
                    ch.op("tensor", lambda tt, c=c, fc=fc, t=t: tt.matmul(
                        out=psh[:], lhsT=sw1T[c][:, fc * 128:(fc + 1) * 128],
                        rhs=lnT[c][:, t * 512:(t + 1) * 512],
                        start=(c == 0), stop=(c == 7)))
                emit_gelu_tanh(nc, ch, (g2, g3, inner, th), psh, hbf, f"l0c_g{fc}_{t}")
                ch.dma("gpsimd", hsT_dram[fc * 128:(fc + 1) * 128, t * 512:(t + 1) * 512],
                       hbf[:])

    with ExitStack() as st2b:
        sw2T = [alloc_sb(nc, st2b, f"l0c_sw2T{c}", [128, D], BF16) for c in range(16)]
        for c in range(16):
            ch.dma_transpose(sw2T[c][:], sw2b_dram[:, c * 128:(c + 1) * 128])
        hstile = alloc_sb(nc, st2b, "l0c_hstile", [128, 16, 128], BF16)
        gix = alloc_sb(nc, st2b, "l0c_gix", [128, 6], U32)
        racc = alloc_sb(nc, st2b, "l0c_racc", [128, D])
        psd = alloc_ps(nc, st2b, "l0c_psd", [128, 512])
        ycmb = alloc_sb(nc, st2b, "l0c_ycmb", [128, D])
        for ti in range(T // 128):
            r0 = ti * 128
            ch.dma("gpsimd", hstile[:],
                   hsT_dram[:, r0:r0 + 128].rearrange("(c p) t -> p c t", p=128))
            ch.dma("gpsimd", gix[:], gidx_dram[r0:r0 + 128, :])
            ch.op("vector", lambda v: v.memset(racc[:], 0.0))
            for r in range(K0):
                ch.dma_fn("gpsimd", lambda g, r=r: g.indirect_dma_start(
                    out=racc[:], out_offset=None, in_=ysbuf,
                    in_offset=bass.IndirectOffsetOnAxis(ap=gix[:, r:r + 1], axis=0),
                    compute_op=ALU.add))
            for jc in range(2):
                for c in range(16):
                    ch.op("tensor", lambda tt, c=c, jc=jc: tt.matmul(
                        out=psd[:], lhsT=hstile[:, c, :],
                        rhs=sw2T[c][:, jc * 512:(jc + 1) * 512],
                        start=(c == 0), stop=(c == 15)))
                ch.op("vector", lambda v, jc=jc: v.tensor_tensor(
                    out=ycmb[:, jc * 512:(jc + 1) * 512], in0=psd[:],
                    in1=racc[:, jc * 512:(jc + 1) * 512], op=ALU.add))
            ch.dma("gpsimd", y_dram[r0:r0 + 128, :], ycmb[:])

    with ExitStack() as st3:
        emit_ln_tok(nc, ch, st3, "l0c_ln2", y_dram, None, res_add_dram=x1_dram,
                    x_out_dram=x2_dram)



def _mk(nc, name, shape, dt, kind=None):
    if kind:
        return nc.dram_tensor(name, shape, dt, kind=kind).ap()
    return nc.dram_tensor(name, shape, dt).ap()


def build_launch_l0ffn_route():
    nc = bass.Bass("TRN2", target_bir_lowering=False)
    x1 = _mk(nc, "x1", [T, D], F32, "ExternalInput")
    rw = _mk(nc, "rw", [E0, D], F32, "ExternalInput")
    eb = _mk(nc, "eb", [E0], F32, "ExternalInput")
    ln1 = _mk(nc, "ln1", [T, D], BF16, "ExternalOutput")
    idx6 = _mk(nc, "idx6", [T, 8], U32, "ExternalOutput")
    w64 = _mk(nc, "w64", [T, E0], F32, "ExternalOutput")
    totals = _mk(nc, "totals", [E0, 1], F32, "ExternalOutput")
    scratch = {"rwb": _mk(nc, "s_rwb", [E0, D], BF16)}
    ch = Chain()
    with ExitStack() as st:
        sems = [st.enter_context(nc.semaphore(n)) for n in ("s_gp", "s_hw", "s_cc", "s_eng")]
        block = st.enter_context(nc.Block())
        emit_l0ffn_routing(nc, ch, st, x1, rw, eb, ln1, idx6, w64, totals, scratch)
        ch.emit(block, sems)
    return nc


def build_launch_l0ffn_scatter():
    nc = bass.Bass("TRN2", target_bir_lowering=False)
    ln1 = _mk(nc, "ln1", [T, D], BF16, "ExternalInput")
    idx6 = _mk(nc, "idx6", [T, 8], U32, "ExternalInput")
    w64 = _mk(nc, "w64", [T, E0], F32, "ExternalInput")
    totals_all = _mk(nc, "totals_all", [NCORES, E0], F32, "ExternalInput")
    prevmask = _mk(nc, "prevmask", [NCORES, 1], F32, "ExternalInput")
    slotbuf = _mk(nc, "slotbuf", [NSLOT + 128, ROWW], BF16, "ExternalOutput")
    gidx = _mk(nc, "gidx", [T, 6], U32, "ExternalOutput")
    ch = Chain()
    with ExitStack() as st:
        sems = [st.enter_context(nc.semaphore(n)) for n in ("s_gp", "s_hw", "s_cc", "s_eng")]
        block = st.enter_context(nc.Block())
        zt = alloc_sb(nc, st, "l0s_zt", [128, ROWW], BF16)
        ch.op("vector", lambda v: v.memset(zt[:], 0.0))
        nrep = NSLOT // 128 + 1
        half = (nrep + 1) // 2
        for hh in range(2):
            lo, hi = hh * half, min((hh + 1) * half, nrep)
            ch.dma("gpsimd",
                   slotbuf[lo * 128:hi * 128, :]
                   .rearrange("(n p) w -> p n w", p=128),
                   zt[:].unsqueeze(1).broadcast_to((128, hi - lo, ROWW)))
        emit_l0ffn_scatter(nc, ch, st, ln1, idx6, w64, totals_all, prevmask, slotbuf, gidx)
        ch.emit(block, sems)
    return nc


def build_launch_l0ffn_experts():
    nc = bass.Bass("TRN2", target_bir_lowering=False)
    parts = _mk(nc, "parts", [NCORES, 8 * CAP, ROWW], BF16, "ExternalInput")
    w1 = _mk(nc, "w1", [8, D, 512], F32, "ExternalInput")
    w2 = _mk(nc, "w2", [8, 512, D], F32, "ExternalInput")
    ys = _mk(nc, "ys", [8 * CAP, D], BF16, "ExternalOutput")
    scratch = {"xs": _mk(nc, "s_xs", [8 * CAP, D], BF16)}
    ch = Chain()
    with ExitStack() as st:
        sems = [st.enter_context(nc.semaphore(n)) for n in ("s_gp", "s_hw", "s_cc", "s_eng")]
        block = st.enter_context(nc.Block())
        emit_l0ffn_experts(nc, ch, st, parts, w1, w2, ys, scratch)
        ch.emit(block, sems)
    return nc


def build_launch_l0ffn_combine():
    nc = bass.Bass("TRN2", target_bir_lowering=False)
    x1 = _mk(nc, "x1", [T, D], F32, "ExternalInput")
    ln1 = _mk(nc, "ln1", [T, D], BF16, "ExternalInput")
    gidx = _mk(nc, "gidx", [T, 6], U32, "ExternalInput")
    ysall = _mk(nc, "ysall", [NSLOT, D], BF16, "ExternalInput")
    sw1 = _mk(nc, "sw1", [2048, D], F32, "ExternalInput")
    sw2 = _mk(nc, "sw2", [D, 2048], F32, "ExternalInput")
    x2 = _mk(nc, "x2", [T, D], F32, "ExternalOutput")
    scratch = {
        "ysbuf": _mk(nc, "s_ysbuf", [NSLOT + 1, D], BF16),
        "yffn": _mk(nc, "s_yffn", [T, D], F32),
        "sw1b": _mk(nc, "s_sw1b", [2048, D], BF16),
        "sw2b": _mk(nc, "s_sw2b", [D, 2048], BF16),
        "hsT": _mk(nc, "s_hsT", [2048, T], BF16),
    }
    ch = Chain()
    with ExitStack() as st:
        sems = [st.enter_context(nc.semaphore(n)) for n in ("s_gp", "s_hw", "s_cc", "s_eng")]
        block = st.enter_context(nc.Block())
        emit_l0ffn_combine(nc, ch, st, x1, ln1, gidx, ysall, sw1, sw2, x2, scratch)
        ch.emit(block, sems)
    return nc


LCH = 256


def emit_l1_attn(nc, ch, st, x2_dram, wqkv_dram, bqkv_dram, wout_dram, bout_dram,
                 x3_dram, scratch):
    ln_dram = scratch["l1ln"]
    pqT_dram = scratch["pqT"]
    pkT_dram = scratch["pkT"]
    ktok_dram = scratch["ktok"]
    v_dram = scratch["l1v"]
    oT_dram = scratch["l1oT"]
    y_dram = scratch["l1y"]

    with ExitStack() as st1:
        emit_ln_tok(nc, ch, st1, "l1a_ln1", x2_dram, ln_dram)

    with ExitStack() as st2:
        xT = [alloc_sb(nc, st2, f"l1a_xT{c}", [128, T], BF16) for c in range(8)]
        emit_transpose_to_feat(nc, ch, ln_dram, xT)
        wat = [alloc_sb(nc, st2, f"l1a_w{c}", [128, 3072], BF16) for c in range(8)]
        for c in range(8):
            ch.dma("gpsimd", wat[c][:], wqkv_dram[c * 128:(c + 1) * 128, :])
        bcol = alloc_sb(nc, st2, "l1a_bcol", [128, 1])
        bcol8 = alloc_sb(nc, st2, "l1a_bcol8", [128, 1])
        ps = alloc_ps(nc, st2, "l1a_ps", [128, 512])
        qf = alloc_sb(nc, st2, "l1a_qf", [128, 512])
        zneg = alloc_sb(nc, st2, "l1a_zneg", [128, 512])
        zexp = alloc_sb(nc, st2, "l1a_zexp", [128, 512])
        zpos = alloc_sb(nc, st2, "l1a_zpos", [128, 512])
        phib = alloc_sb(nc, st2, "l1a_phib", [128, 512], BF16)
        evb = alloc_sb(nc, st2, "l1a_evb", [128, 512], BF16)

        for (which, col0, dstT) in (("q", 0, pqT_dram), ("k", D, pkT_dram)):
            scale = 0.125 if which == "q" else 1.0
            for j in range(8):
                ch.dma("gpsimd", bcol[:], bqkv_dram[col0 + j * 128:col0 + (j + 1) * 128]
                       .rearrange("(p o) -> p o", o=1))
                ch.op("vector", lambda v, scale=scale: v.tensor_scalar(
                    out=bcol8[:], in0=bcol[:], scalar1=scale, scalar2=None, op0=ALU.mult))
                for t in range(T // 512):
                    for c in range(8):
                        ch.op("tensor", lambda tt, c=c, j=j, t=t, col0=col0: tt.matmul(
                            out=ps[:], lhsT=wat[c][:, col0 + j * 128:col0 + (j + 1) * 128],
                            rhs=xT[c][:, t * 512:(t + 1) * 512],
                            start=(c == 0), stop=(c == 7)))
                    ch.op("vector", lambda v, scale=scale: v.tensor_scalar(
                        out=qf[:], in0=ps[:], scalar1=scale, scalar2=bcol8[:, 0:1],
                        op0=ALU.mult, op1=ALU.add))
                    ch.op("vector", lambda v: v.tensor_scalar(
                        out=zneg[:], in0=qf[:], scalar1=0.0, scalar2=None, op0=ALU.min))
                    ch.op("scalar", lambda s: s.activation(out=zexp[:], in_=zneg[:], func=AF.Exp))
                    ch.op("vector", lambda v: v.tensor_scalar(
                        out=zpos[:], in0=qf[:], scalar1=0.0, scalar2=None, op0=ALU.max))
                    ch.op("vector", lambda v: v.scalar_tensor_tensor(
                        out=phib[:], in0=zpos[:], scalar=1e-4, in1=zexp[:],
                        op0=ALU.add, op1=ALU.add))
                    ch.dma("gpsimd", dstT[j * 128:(j + 1) * 128, t * 512:(t + 1) * 512],
                           phib[:])

        vbrow = alloc_sb(nc, st2, "l1a_vbrow", [1, D])
        onesr = alloc_sb(nc, st2, "l1a_onesr", [1, 128])
        psvb = alloc_ps(nc, st2, "l1a_psvb", [128, 512])
        vbias = alloc_sb(nc, st2, "l1a_vbias", [128, D])
        ch.dma("gpsimd", vbrow[:], bqkv_dram[2 * D:3 * D].rearrange("(o d) -> o d", o=1))
        ch.op("vector", lambda v: v.memset(onesr[:], 1.0))
        for g in range(2):
            ch.op("tensor", lambda t, g=g: t.matmul(
                out=psvb[:], lhsT=onesr[:], rhs=vbrow[:, g * 512:(g + 1) * 512],
                start=True, stop=True))
            ch.op("vector", lambda v, g=g: v.tensor_copy(
                out=vbias[:, g * 512:(g + 1) * 512], in_=psvb[:]))
        for g in range(2):
            for ti in range(T // 128):
                for c in range(8):
                    ch.op("tensor", lambda tt, c=c, ti=ti, g=g: tt.matmul(
                        out=ps[:], lhsT=xT[c][:, ti * 128:(ti + 1) * 128],
                        rhs=wat[c][:, 2 * D + g * 512: 2 * D + (g + 1) * 512],
                        start=(c == 0), stop=(c == 7)))
                ch.op("vector", lambda v, g=g: v.tensor_tensor(
                    out=qf[:], in0=ps[:], in1=vbias[:, g * 512:(g + 1) * 512], op=ALU.add))
                ch.op("vector", lambda v: v.tensor_copy(out=evb[:], in_=qf[:]))
                ch.dma("gpsimd", v_dram[ti, 8 * g:8 * (g + 1), :, :].rearrange("h p c -> p h c"),
                       evb[:])

        ktb = alloc_sb(nc, st2, "l1a_ktb", [128, D], BF16)
        for ti in range(T // 128):
            ch.dma_transpose(ktb[:], pkT_dram[:, ti * 128:(ti + 1) * 128])
            ch.dma("gpsimd", ktok_dram[ti * 128:(ti + 1) * 128, :], ktb[:])

    with ExitStack() as st3:
        NCH = T // LCH
        SPC = LCH // 128
        pq = alloc_sb(nc, st3, "l1a_pq", [64, T], BF16)
        pk = alloc_sb(nc, st3, "l1a_pk", [64, T], BF16)
        vt = alloc_sb(nc, st3, "l1a_vt", [128, (T // 128) * 65], BF16)
        kt = alloc_sb(nc, st3, "l1a_kt", [128, (T // 128) * 64], BF16)
        kv = alloc_sb(nc, st3, "l1a_kv", [64, 65])
        kvb = alloc_sb(nc, st3, "l1a_kvb", [64, 65], BF16)
        masks = alloc_sb(nc, st3, "l1a_masks", [128, SPC * LCH], BF16)
        colid = alloc_sb(nc, st3, "l1a_colid", [128, LCH])
        rowid = alloc_sb(nc, st3, "l1a_rowid", [128, 1])
        psa = alloc_ps(nc, st3, "l1a_psa", [128, LCH])
        ea = [alloc_sb(nc, st3, f"l1a_ea{i}", [128, LCH], BF16) for i in range(SPC)]
        pso = alloc_ps(nc, st3, "l1a_pso", [65, LCH])
        pskv = alloc_ps(nc, st3, "l1a_pskv", [64, 65])
        rd = alloc_sb(nc, st3, "l1a_rd", [1, LCH])
        ones64 = alloc_sb(nc, st3, "l1a_ones64", [1, 64])
        psbc = alloc_ps(nc, st3, "l1a_psbc", [64, LCH])
        bcb = alloc_sb(nc, st3, "l1a_bcb", [64, LCH])
        obn = alloc_sb(nc, st3, "l1a_obn", [64, LCH], BF16)

        ch.op("gpsimd", lambda g: g.iota(colid[:], pattern=[[1, LCH]], base=0,
                                         channel_multiplier=0,
                                         allow_small_or_imprecise_dtypes=True))
        ch.op("gpsimd", lambda g: g.iota(rowid[:], pattern=[[0, 1]], base=0,
                                         channel_multiplier=1,
                                         allow_small_or_imprecise_dtypes=True))
        for i in range(SPC):
            ch.op("vector", lambda v, i=i: v.tensor_scalar(
                out=masks[:, i * LCH:(i + 1) * LCH], in0=colid[:],
                scalar1=float(128 * i), scalar2=rowid[:, 0:1],
                op0=ALU.subtract, op1=ALU.is_ge))
        ch.op("vector", lambda v: v.memset(ones64[:], 1.0))

        for h in range(NH):
            ch.dma("gpsimd", pq[:], pqT_dram[h * 64:(h + 1) * 64, :])
            ch.dma("gpsimd", pk[:], pkT_dram[h * 64:(h + 1) * 64, :])
            ch.op("vector", lambda v: v.memset(vt[:], 1.0))
            ch.dma("gpsimd", vt[:].rearrange("p (k c) -> p k c", c=65)[:, :, 0:64],
                   v_dram[:, h, :, :].rearrange("k p c -> p k c"))
            ch.dma("gpsimd", kt[:].rearrange("p (k c) -> p k c", c=64),
                   ktok_dram.rearrange("(k p) (hh c) -> p k hh c", p=128, c=64)[:, :, h, :])
            ch.op("vector", lambda v: v.memset(kv[:], 0.0))
            ch.op("vector", lambda v: v.memset(kvb[:], 0.0))
            for cc in range(NCH):
                q0 = cc * LCH
                for i in range(SPC):
                    s0 = (cc * SPC + i) * 128
                    ch.op("tensor", lambda tt, s0=s0, q0=q0: tt.matmul(
                        out=psa[:], lhsT=pk[:, s0:s0 + 128], rhs=pq[:, q0:q0 + LCH],
                        start=True, stop=True))
                    ch.op("vector", lambda v, i=i: v.tensor_tensor(
                        out=ea[i][:], in0=psa[:], in1=masks[:, i * LCH:(i + 1) * LCH],
                        op=ALU.mult))
                for i in range(SPC):
                    kti = cc * SPC + i
                    ch.op("tensor", lambda tt, i=i, kti=kti: tt.matmul(
                        out=pso[:], lhsT=vt[:, kti * 65:(kti + 1) * 65], rhs=ea[i][:],
                        start=(i == 0), stop=False))
                ch.op("tensor", lambda tt, q0=q0: tt.matmul(
                    out=pso[:], lhsT=kvb[:], rhs=pq[:, q0:q0 + LCH],
                    start=False, stop=True))
                ch.op("vector", lambda v: v.tensor_scalar(
                    out=rd[:], in0=pso[64:65, :], scalar1=1e-6, scalar2=None, op0=ALU.add))
                ch.op("vector", lambda v: v.reciprocal(out=rd[:], in_=rd[:]))
                ch.op("tensor", lambda tt: tt.matmul(out=psbc[:], lhsT=ones64[:], rhs=rd[:],
                                                     start=True, stop=True))
                ch.op("vector", lambda v: v.tensor_copy(out=bcb[:], in_=psbc[:]))
                ch.op("vector", lambda v: v.tensor_tensor(out=obn[:], in0=pso[0:64, :],
                                                          in1=bcb[:], op=ALU.mult))
                ch.dma("gpsimd", oT_dram[h * 64:(h + 1) * 64, q0:q0 + LCH], obn[:])
                for i in range(SPC):
                    kti = cc * SPC + i
                    ch.op("tensor", lambda tt, i=i, kti=kti: tt.matmul(
                        out=pskv[:], lhsT=kt[:, kti * 64:(kti + 1) * 64],
                        rhs=vt[:, kti * 65:(kti + 1) * 65],
                        start=(i == 0), stop=(i == SPC - 1)))
                ch.op("vector", lambda v: v.tensor_tensor(out=kv[:], in0=kv[:], in1=pskv[:],
                                                          op=ALU.add))
                ch.op("vector", lambda v: v.tensor_copy(out=kvb[:], in_=kv[:]))

    with ExitStack() as st4:
        oT = [alloc_sb(nc, st4, f"l1a_oT{c}", [128, T], BF16) for c in range(8)]
        for c in range(8):
            ch.dma("gpsimd", oT[c][:], oT_dram[c * 128:(c + 1) * 128, :])
        wp = [alloc_sb(nc, st4, f"l1a_wp{c}", [128, D], BF16) for c in range(8)]
        for c in range(8):
            ch.dma("gpsimd", wp[c][:], wout_dram[c * 128:(c + 1) * 128, :])
        borow = alloc_sb(nc, st4, "l1a_borow", [1, D])
        ones_r = alloc_sb(nc, st4, "l1a_onesr2", [1, 128])
        psbb = alloc_ps(nc, st4, "l1a_psbb", [128, 512])
        bob = alloc_sb(nc, st4, "l1a_bob", [128, D])
        ch.dma("gpsimd", borow[:], bout_dram.rearrange("(o d) -> o d", o=1))
        ch.op("vector", lambda v: v.memset(ones_r[:], 1.0))
        for jc in range(2):
            ch.op("tensor", lambda t, jc=jc: t.matmul(
                out=psbb[:], lhsT=ones_r[:], rhs=borow[:, jc * 512:(jc + 1) * 512],
                start=True, stop=True))
            ch.op("vector", lambda v, jc=jc: v.tensor_copy(
                out=bob[:, jc * 512:(jc + 1) * 512], in_=psbb[:]))
        ps2 = alloc_ps(nc, st4, "l1a_ps2", [128, 512])
        yb = alloc_sb(nc, st4, "l1a_yb", [128, 512])
        for ti in range(T // 128):
            for jc in range(2):
                for c in range(8):
                    ch.op("tensor", lambda tt, c=c, ti=ti, jc=jc: tt.matmul(
                        out=ps2[:], lhsT=oT[c][:, ti * 128:(ti + 1) * 128],
                        rhs=wp[c][:, jc * 512:(jc + 1) * 512],
                        start=(c == 0), stop=(c == 7)))
                ch.op("vector", lambda v, jc=jc: v.tensor_tensor(
                    out=yb[:], in0=ps2[:], in1=bob[:, jc * 512:(jc + 1) * 512], op=ALU.add))
                ch.dma("gpsimd", y_dram[ti * 128:(ti + 1) * 128,
                                        jc * 512:(jc + 1) * 512], yb[:])

    with ExitStack() as st5:
        emit_ln_tok(nc, ch, st5, "l1a_ln2", y_dram, None, res_add_dram=x2_dram,
                    x_out_dram=x3_dram)


def build_launch_l1attn():
    nc = bass.Bass("TRN2", target_bir_lowering=False)
    x2 = _mk(nc, "x2", [T, D], F32, "ExternalInput")
    wqkv = _mk(nc, "wqkv", [D, 3 * D], F32, "ExternalInput")
    bqkv = _mk(nc, "bqkv", [3 * D], F32, "ExternalInput")
    wout = _mk(nc, "wout", [D, D], F32, "ExternalInput")
    bout = _mk(nc, "bout", [D], F32, "ExternalInput")
    x3 = _mk(nc, "x3", [T, D], F32, "ExternalOutput")
    scratch = {
        "l1ln": _mk(nc, "s_l1ln", [T, D], BF16),
        "pqT": _mk(nc, "s_pqT", [D, T], BF16),
        "pkT": _mk(nc, "s_pkT", [D, T], BF16),
        "ktok": _mk(nc, "s_ktok", [T, D], BF16),
        "l1v": _mk(nc, "s_l1v", [T // 128, 16, 128, 64], BF16),
        "l1oT": _mk(nc, "s_l1oT", [D, T], BF16),
        "l1y": _mk(nc, "s_l1y", [T, D], F32),
    }
    ch = Chain()
    with ExitStack() as st:
        sems = [st.enter_context(nc.semaphore(n)) for n in ("s_gp", "s_hw", "s_cc", "s_eng")]
        block = st.enter_context(nc.Block())
        emit_l1_attn(nc, ch, st, x2, wqkv, bqkv, wout, bout, x3, scratch)
        ch.emit(block, sems)
    return nc


E1 = 8
K1 = 4


def emit_l1_ffn(nc, ch, st, x3_dram, rw_dram, gw_dram, uw_dram, dw_dram,
                sgw_dram, suw_dram, sdw_dram, xout_dram, scratch):
    ln_dram = scratch["l2ln"]
    actT_dram = scratch["actT"]
    hsT_dram = scratch["hsT2"]
    y_dram = scratch["l2y"]

    w8all = alloc_sb(nc, st, "l2_w8all", [128, (T // 128) * E1])

    with ExitStack() as st1:
        emit_ln_tok(nc, ch, st1, "l2_ln1", x3_dram, ln_dram)

    with ExitStack() as st2:
        lnT = [alloc_sb(nc, st2, f"l2_lnT{c}", [128, T], BF16) for c in range(8)]
        emit_transpose_to_feat(nc, ch, ln_dram, lnT)

        rwb = [alloc_sb(nc, st2, f"l2_rwb{c}", [128, E1], BF16) for c in range(8)]
        for c in range(8):
            ch.dma("gpsimd", rwb[c][:], rw_dram[c * 128:(c + 1) * 128, :])
        psr = alloc_ps(nc, st2, "l2_psr", [128, E1])
        nmx = alloc_sb(nc, st2, "l2_nmx", [128, 1])
        exr = alloc_sb(nc, st2, "l2_exr", [128, E1])
        ssum = alloc_sb(nc, st2, "l2_ssum", [128, 1])
        srec = alloc_sb(nc, st2, "l2_srec", [128, 1])
        scr = alloc_sb(nc, st2, "l2_scr", [128, E1])
        mx8 = alloc_sb(nc, st2, "l2_mx8", [128, 8])
        mi8 = alloc_sb(nc, st2, "l2_mi8", [128, 8], U32)
        mif = alloc_sb(nc, st2, "l2_mif", [128, 8])
        w4s = alloc_sb(nc, st2, "l2_w4s", [128, 1])
        w4r = alloc_sb(nc, st2, "l2_w4r", [128, 1])
        w4n = alloc_sb(nc, st2, "l2_w4n", [128, K1])
        iota8 = alloc_sb(nc, st2, "l2_iota8", [128, E1])
        ohw = alloc_sb(nc, st2, "l2_ohw", [128, E1])
        w8 = alloc_sb(nc, st2, "l2_w8", [128, E1])
        ch.op("gpsimd", lambda g: g.iota(iota8[:], pattern=[[1, E1]], base=0,
                                         channel_multiplier=0,
                                         allow_small_or_imprecise_dtypes=True))
        for ti in range(T // 128):
            for c in range(8):
                ch.op("tensor", lambda t, c=c, ti=ti: t.matmul(
                    out=psr[:], lhsT=lnT[c][:, ti * 128:(ti + 1) * 128], rhs=rwb[c][:],
                    start=(c == 0), stop=(c == 7)))
            ch.op("vector", lambda v: v.tensor_reduce(out=nmx[:], in_=psr[:],
                                                      axis=AX.X, op=ALU.max, negate=True))
            ch.op("scalar", lambda s: s.activation(out=exr[:], in_=psr[:], func=AF.Exp,
                                                   bias=nmx[:, 0:1]))
            ch.op("vector", lambda v: v.tensor_reduce(out=ssum[:], in_=exr[:],
                                                      axis=AX.X, op=ALU.add))
            ch.op("vector", lambda v: v.reciprocal(out=srec[:], in_=ssum[:]))
            ch.op("vector", lambda v: v.tensor_scalar(out=scr[:], in0=exr[:],
                                                      scalar1=srec[:, 0:1], scalar2=None,
                                                      op0=ALU.mult))
            ch.op("vector", lambda v: v.max(out=mx8[:], in_=scr[:]))
            ch.op("vector", lambda v: v.max_index(out=mi8[:], in_max=mx8[:], in_values=scr[:]))
            ch.op("vector", lambda v: v.tensor_copy(out=mif[:], in_=mi8[:]))
            ch.op("vector", lambda v: v.tensor_reduce(out=w4s[:], in_=mx8[:, 0:K1],
                                                      axis=AX.X, op=ALU.add))
            ch.op("vector", lambda v: v.tensor_scalar(out=w4s[:], in0=w4s[:],
                                                      scalar1=1e-20, scalar2=None, op0=ALU.add))
            ch.op("vector", lambda v: v.reciprocal(out=w4r[:], in_=w4s[:]))
            ch.op("vector", lambda v: v.tensor_scalar(out=w4n[:], in0=mx8[:, 0:K1],
                                                      scalar1=w4r[:, 0:1], scalar2=None,
                                                      op0=ALU.mult))
            for r in range(K1):
                if r == 0:
                    ch.op("vector", lambda v, r=r: v.tensor_scalar(
                        out=w8[:], in0=iota8[:], scalar1=mif[:, r:r + 1],
                        scalar2=w4n[:, r:r + 1], op0=ALU.is_equal, op1=ALU.mult))
                else:
                    ch.op("vector", lambda v, r=r: v.tensor_scalar(
                        out=ohw[:], in0=iota8[:], scalar1=mif[:, r:r + 1],
                        scalar2=w4n[:, r:r + 1], op0=ALU.is_equal, op1=ALU.mult))
                    ch.op("vector", lambda v: v.tensor_tensor(out=w8[:], in0=w8[:],
                                                              in1=ohw[:], op=ALU.add))
            ch.op("vector", lambda v, ti=ti: v.tensor_copy(
                out=w8all[:, ti * E1:(ti + 1) * E1], in_=w8[:]))

        gwb = [alloc_sb(nc, st2, f"l2_gwb{c}", [128, 512], BF16) for c in range(8)]
        uwb = [alloc_sb(nc, st2, f"l2_uwb{c}", [128, 512], BF16) for c in range(8)]
        psg = alloc_ps(nc, st2, "l2_psg", [128, 512])
        psu = alloc_ps(nc, st2, "l2_psu", [128, 512])
        sg = alloc_sb(nc, st2, "l2_sg", [128, 512])
        actb = alloc_sb(nc, st2, "l2_actb", [128, 512], BF16)
        for e in range(E1):
            for c in range(8):
                ch.dma("gpsimd", gwb[c][:], gw_dram[e, c * 128:(c + 1) * 128, :])
                ch.dma("gpsimd", uwb[c][:], uw_dram[e, c * 128:(c + 1) * 128, :])
            for ic in range(4):
                for t in range(T // 512):
                    for c in range(8):
                        ch.op("tensor", lambda tt, c=c, ic=ic, t=t: tt.matmul(
                            out=psg[:], lhsT=gwb[c][:, ic * 128:(ic + 1) * 128],
                            rhs=lnT[c][:, t * 512:(t + 1) * 512],
                            start=(c == 0), stop=(c == 7)))
                    for c in range(8):
                        ch.op("tensor", lambda tt, c=c, ic=ic, t=t: tt.matmul(
                            out=psu[:], lhsT=uwb[c][:, ic * 128:(ic + 1) * 128],
                            rhs=lnT[c][:, t * 512:(t + 1) * 512],
                            start=(c == 0), stop=(c == 7)))
                    ch.op("scalar", lambda s: s.activation(out=sg[:], in_=psg[:], func=AF.Silu))
                    ch.op("vector", lambda v: v.tensor_tensor(out=actb[:], in0=sg[:],
                                                              in1=psu[:], op=ALU.mult))
                    ch.dma("gpsimd", actT_dram[e, ic * 128:(ic + 1) * 128,
                                               t * 512:(t + 1) * 512], actb[:])

        sgw = [alloc_sb(nc, st2, f"l2_sgw{c}", [128, 2048], BF16) for c in range(8)]
        suw = [alloc_sb(nc, st2, f"l2_suw{c}", [128, 2048], BF16) for c in range(8)]
        for c in range(8):
            ch.dma("gpsimd", sgw[c][:], sgw_dram[c * 128:(c + 1) * 128, :])
            ch.dma("gpsimd", suw[c][:], suw_dram[c * 128:(c + 1) * 128, :])
        for mc in range(16):
            for t in range(T // 512):
                for c in range(8):
                    ch.op("tensor", lambda tt, c=c, mc=mc, t=t: tt.matmul(
                        out=psg[:], lhsT=sgw[c][:, mc * 128:(mc + 1) * 128],
                        rhs=lnT[c][:, t * 512:(t + 1) * 512],
                        start=(c == 0), stop=(c == 7)))
                for c in range(8):
                    ch.op("tensor", lambda tt, c=c, mc=mc, t=t: tt.matmul(
                        out=psu[:], lhsT=suw[c][:, mc * 128:(mc + 1) * 128],
                        rhs=lnT[c][:, t * 512:(t + 1) * 512],
                        start=(c == 0), stop=(c == 7)))
                ch.op("scalar", lambda s: s.activation(out=sg[:], in_=psg[:], func=AF.Silu))
                ch.op("vector", lambda v: v.tensor_tensor(out=actb[:], in0=sg[:],
                                                          in1=psu[:], op=ALU.mult))
                ch.dma("gpsimd", hsT_dram[mc * 128:(mc + 1) * 128,
                                          t * 512:(t + 1) * 512], actb[:])

    with ExitStack() as st3:
        dwb = [[alloc_sb(nc, st3, f"l2_dwb{e}_{c}", [128, D], BF16) for c in range(4)]
               for e in range(E1)]
        for e in range(E1):
            for c in range(4):
                ch.dma("gpsimd", dwb[e][c][:], dw_dram[e, c * 128:(c + 1) * 128, :])
        sdw = [alloc_sb(nc, st3, f"l2_sdw{c}", [128, D], BF16) for c in range(16)]
        for c in range(16):
            ch.dma("gpsimd", sdw[c][:], sdw_dram[c * 128:(c + 1) * 128, :])
        atile = alloc_sb(nc, st3, "l2_atile", [128, E1, 4, 128], BF16)
        hstile = alloc_sb(nc, st3, "l2_hstile", [128, 16, 128], BF16)
        psd = alloc_ps(nc, st3, "l2_psd", [128, 512])
        pss = alloc_ps(nc, st3, "l2_pss", [128, 512])
        rsum = alloc_sb(nc, st3, "l2_rsum", [128, 512])
        yb = alloc_sb(nc, st3, "l2_yb", [128, 512])
        for ti in range(T // 128):
            r0 = ti * 128
            ch.dma("gpsimd", atile[:],
                   actT_dram[:, :, r0:r0 + 128].rearrange("e (c p) t -> p e c t", p=128))
            ch.dma("gpsimd", hstile[:],
                   hsT_dram[:, r0:r0 + 128].rearrange("(c p) t -> p c t", p=128))
            for jc in range(2):
                for e in range(E1):
                    for ic in range(4):
                        ch.op("tensor", lambda tt, e=e, ic=ic, jc=jc: tt.matmul(
                            out=psd[:], lhsT=atile[:, e, ic, :],
                            rhs=dwb[e][ic][:, jc * 512:(jc + 1) * 512],
                            start=(ic == 0), stop=(ic == 3)))
                    if e == 0:
                        ch.op("vector", lambda v, ti=ti, e=e: v.tensor_scalar(
                            out=rsum[:], in0=psd[:],
                            scalar1=w8all[:, ti * E1 + e:ti * E1 + e + 1], scalar2=None,
                            op0=ALU.mult))
                    else:
                        ch.op("vector", lambda v, ti=ti, e=e: v.scalar_tensor_tensor(
                            out=rsum[:], in0=psd[:],
                            scalar=w8all[:, ti * E1 + e:ti * E1 + e + 1],
                            in1=rsum[:], op0=ALU.mult, op1=ALU.add))
                for c in range(16):
                    ch.op("tensor", lambda tt, c=c, jc=jc: tt.matmul(
                        out=pss[:], lhsT=hstile[:, c, :],
                        rhs=sdw[c][:, jc * 512:(jc + 1) * 512],
                        start=(c == 0), stop=(c == 15)))
                ch.op("vector", lambda v: v.tensor_tensor(out=yb[:], in0=pss[:],
                                                          in1=rsum[:], op=ALU.add))
                ch.dma("gpsimd", y_dram[r0:r0 + 128, jc * 512:(jc + 1) * 512], yb[:])

    with ExitStack() as st4:
        emit_ln_tok(nc, ch, st4, "l2_ln2", y_dram, None, res_add_dram=x3_dram,
                    x_out_dram=xout_dram)


def build_launch_l1ffn():
    nc = bass.Bass("TRN2", target_bir_lowering=False)
    x3 = _mk(nc, "x3", [T, D], F32, "ExternalInput")
    rw = _mk(nc, "rw", [D, E1], F32, "ExternalInput")
    gw = _mk(nc, "gw", [E1, D, 512], F32, "ExternalInput")
    uw = _mk(nc, "uw", [E1, D, 512], F32, "ExternalInput")
    dw = _mk(nc, "dw", [E1, 512, D], F32, "ExternalInput")
    sgw = _mk(nc, "sgw", [D, 2048], F32, "ExternalInput")
    suw = _mk(nc, "suw", [D, 2048], F32, "ExternalInput")
    sdw = _mk(nc, "sdw", [2048, D], F32, "ExternalInput")
    xout = _mk(nc, "xout", [T, D], F32, "ExternalOutput")
    scratch = {
        "l2ln": _mk(nc, "s_l2ln", [T, D], BF16),
        "actT": _mk(nc, "s_actT", [E1, 512, T], BF16),
        "hsT2": _mk(nc, "s_hsT2", [2048, T], BF16),
        "l2y": _mk(nc, "s_l2y", [T, D], F32),
    }
    ch = Chain()
    with ExitStack() as st:
        sems = [st.enter_context(nc.semaphore(n)) for n in ("s_gp", "s_hw", "s_cc", "s_eng")]
        block = st.enter_context(nc.Block())
        emit_l1_ffn(nc, ch, st, x3, rw, gw, uw, dw, sgw, suw, sdw, xout, scratch)
        ch.emit(block, sems)
    return nc


_BUILD_CACHE = {}


def _get(name, fn):
    if name not in _BUILD_CACHE:
        _BUILD_CACHE[name] = fn()
    return _BUILD_CACHE[name]


def kernel(**inputs):
    cores = list(range(NCORES))
    x = np.ascontiguousarray(inputs["L0_attn_x"])

    nc1 = _get("l0attn", build_launch_l0attn)
    r1 = run_bass_kernel_spmd(nc1, [
        {"x": x[i], "w_attn": inputs["L0_attn_w_attn"],
         "w_proj": inputs["L0_attn_w_proj"]} for i in cores], core_ids=cores).results
    x1 = [r["x1"] for r in r1]

    nc2 = _get("l0route", build_launch_l0ffn_route)
    r2 = run_bass_kernel_spmd(nc2, [
        {"x1": x1[i], "rw": inputs["L0_ffn_router_weight"],
         "eb": inputs["L0_ffn_expert_bias"]} for i in cores], core_ids=cores).results
    totals_all = np.stack([r["totals"].reshape(E0) for r in r2], 0)

    nc3 = _get("l0scat", build_launch_l0ffn_scatter)
    ins3 = []
    for i in cores:
        pm = np.zeros((NCORES, 1), np.float32)
        pm[:i] = 1.0
        ins3.append({"ln1": r2[i]["ln1"], "idx6": r2[i]["idx6"], "w64": r2[i]["w64"],
                     "totals_all": totals_all, "prevmask": pm})
    r3 = run_bass_kernel_spmd(nc3, ins3, core_ids=cores).results

    nc4 = _get("l0exp", build_launch_l0ffn_experts)
    ins4 = []
    for c in cores:
        parts = np.stack([r3[p]["slotbuf"][c * 8 * CAP:(c + 1) * 8 * CAP]
                          for p in cores], 0)
        ins4.append({"parts": parts, "w1": inputs["L0_ffn_w1"][c * 8:(c + 1) * 8],
                     "w2": inputs["L0_ffn_w2"][c * 8:(c + 1) * 8]})
    r4 = run_bass_kernel_spmd(nc4, ins4, core_ids=cores).results
    ysall = np.concatenate([r["ys"] for r in r4], 0)

    nc5 = _get("l0comb", build_launch_l0ffn_combine)
    ins5 = [{"x1": x1[i], "ln1": r2[i]["ln1"], "gidx": r3[i]["gidx"],
             "ysall": ysall, "sw1": inputs["L0_ffn_sw1"],
             "sw2": inputs["L0_ffn_sw2"]} for i in cores]
    r5 = run_bass_kernel_spmd(nc5, ins5, core_ids=cores).results
    x2 = [r["x2"] for r in r5]

    nc6 = _get("l1attn", build_launch_l1attn)
    r6 = run_bass_kernel_spmd(nc6, [
        {"x2": x2[i], "wqkv": inputs["L1_attn_W_qkv"], "bqkv": inputs["L1_attn_b_qkv"],
         "wout": inputs["L1_attn_W_out"], "bout": inputs["L1_attn_b_out"]}
        for i in cores], core_ids=cores).results
    x3 = [r["x3"] for r in r6]

    nc7 = _get("l1ffn", build_launch_l1ffn)
    r7 = run_bass_kernel_spmd(nc7, [
        {"x3": x3[i], "rw": inputs["L1_ffn_router_w"], "gw": inputs["L1_ffn_gate_w"],
         "uw": inputs["L1_ffn_up_w"], "dw": inputs["L1_ffn_down_w"],
         "sgw": inputs["L1_ffn_s_gate_w"], "suw": inputs["L1_ffn_s_up_w"],
         "sdw": inputs["L1_ffn_s_down_w"]} for i in cores], core_ids=cores).results
    return np.stack([r["xout"] for r in r7], 0)
